# version 05 (final)
# baseline (speedup 1.0000x reference)
import contextlib

import numpy as np
import concourse.bass as bass
import concourse.mybir as mybir
from concourse.bass_utils import run_bass_kernel_spmd

F32 = mybir.dt.float32
BF16 = mybir.dt.bfloat16
AF = mybir.ActivationFunctionType
ALU = mybir.AluOpType

NCORES = 8
BLOC = 2
T = 4096
S = BLOC * T
D = 1024
GRP = 512
NGRP = S // GRP
H = 16
HD = 64
CS = 64
DH1 = 4096
DH2 = 2816

PREP_JOBS = None
DBG_PA = False
PA_GROUPS = None
PB_BLOCKS = None
PC_GROUPS = None
PC_SKIP = set()
PD_DBG = False
EPS0 = 1e-5
EPS1 = 1e-6


def host_constants():
    c = {}
    lin = np.linspace(np.float32(2.0), np.float32(8.0), D, dtype=np.float32)
    lam = np.power(np.float32(0.5), (np.float32(1.0) / lin).astype(np.float32)
                   ).astype(np.float32)
    beta = (np.float32(1.0) - lam).astype(np.float32)
    c["c_lam"] = lam.reshape(8, 128).T.copy()
    c["c_beta"] = beta.reshape(8, 128).T.copy()
    c["c_ident"] = np.eye(128, dtype=np.float32)
    c["c_identf"] = np.eye(128, dtype=np.float32)
    c["c_ones_col"] = np.ones((128, 1), np.float32)
    c["c_ones_row"] = np.ones((1, 128), np.float32)
    i = np.arange(CS)
    tril = (i[None, :] < i[:, None]).astype(np.float32)
    c["c_tril"] = np.concatenate([tril, tril], axis=0)
    c["c_triu"] = np.concatenate([tril.T, tril.T], axis=0)
    h2 = np.zeros((128, 2), np.float32)
    h2[:64, 0] = 1.0
    h2[64:, 1] = 1.0
    c["c_h2"] = h2
    s2 = np.zeros((2, 128), np.float32)
    s2[0, :64] = 1.0
    s2[1, 64:] = 1.0
    c["c_s2"] = s2
    S1 = np.zeros((16, 128), np.float32)
    for h in range(16):
        if h % 2 == 0:
            S1[h, :64] = 1.0
        else:
            S1[h, 64:] = 1.0
    c["c_selS1"] = S1
    M = np.zeros((16, 8), np.float32)
    for h in range(16):
        M[h, h // 2] = 1.0
    c["c_m16x8"] = M
    return c


CONST_DTYPES = {"c_ident": BF16}


WEIGHT_NAMES = [
    "L0_attn_Wt", "L0_attn_Wo", "L0_attn_Wg", "L0_attn_bg",
    "L0_attn_ln1_w", "L0_attn_ln1_b", "L0_attn_lnc_w", "L0_attn_lnc_b",
    "L0_attn_lng_w", "L0_attn_lng_b",
    "L0_ffn_w1_w", "L0_ffn_w1_s", "L0_ffn_w2_w", "L0_ffn_w2_s",
    "L0_ffn_h1_w", "L0_ffn_h1_s", "L0_ffn_h2_w", "L0_ffn_h2_s",
    "L0_ffn_w3_w", "L0_ffn_w3_s",
    "L1_attn_Wq", "L1_attn_Wk", "L1_attn_Wv", "L1_attn_Wo",
    "L1_attn_alpha_raw", "L1_attn_beta_raw",
    "L1_ffn_w_gate", "L1_ffn_w_up", "L1_ffn_w_down",
]

INPUT_SHAPES = {
    "x": (S, D),
    "L0_attn_Wt": (D, D), "L0_attn_Wo": (D, D), "L0_attn_Wg": (2 * D, D),
    "L0_attn_bg": (D,),
    "L0_attn_ln1_w": (D,), "L0_attn_ln1_b": (D,),
    "L0_attn_lnc_w": (D,), "L0_attn_lnc_b": (D,),
    "L0_attn_lng_w": (D,), "L0_attn_lng_b": (D,),
    "L0_ffn_w1_w": (DH1, D), "L0_ffn_w1_s": (DH1,),
    "L0_ffn_w2_w": (DH1, D), "L0_ffn_w2_s": (DH1,),
    "L0_ffn_h1_w": (DH1, DH1), "L0_ffn_h1_s": (DH1,),
    "L0_ffn_h2_w": (DH1, DH1), "L0_ffn_h2_s": (DH1,),
    "L0_ffn_w3_w": (D, DH1), "L0_ffn_w3_s": (D,),
    "L1_attn_Wq": (D, D), "L1_attn_Wk": (D, D), "L1_attn_Wv": (D, D),
    "L1_attn_Wo": (D, D),
    "L1_attn_alpha_raw": (1, H, 1, 1), "L1_attn_beta_raw": (1, H, 1, 1),
    "L1_ffn_w_gate": (DH2, D), "L1_ffn_w_up": (DH2, D),
    "L1_ffn_w_down": (D, DH2),
}

SCRATCH = {
    "z_w1_blk": ((DH1 // 128, 128, 8, 128), BF16),
    "z_w2_blk": ((DH1 // 128, 128, 8, 128), BF16),
    "z_h1_blk": ((DH1 // 128, 128, 32, 128), BF16),
    "z_h2_blk": ((DH1 // 128, 128, 32, 128), BF16),
    "z_w3t": ((DH1, D), BF16),
    "z_wgt": ((D, DH2), BF16),
    "z_wut": ((D, DH2), BF16),
    "z_wdt": ((DH2, D), BF16),
    "z_xn1t": ((8, 128, S), BF16),
    "z_sn": ((S, D), F32),
    "z_x2n": ((S, D), F32),
    "z_xn2n": ((S, D), F32),
    "z_xn2t": ((8, 128, S), BF16),
    "z_s2n": ((S, D), F32),
    "z_dbg": ((128, 40, D), F32),
    "z_dxq": ((S, D), BF16),
    "z_dh": ((S, DH2), F32),
    "z_dhq": ((S, DH2), BF16),
    "z_dg2": ((S, D), F32),
}


class Ctx:

    def __init__(self, nc, dram):
        self.nc = nc
        self.d = dram


class SemW:

    def __init__(self, h):
        self.h = h
        self.base = 0

    def edge(self, eng, ins):
        ins.then_inc(self.h, 1)
        self.base += 1
        eng.wait_ge(self.h, self.base)


def qdq_stage1_s(s, xin, tmp, W):
    s.activation(tmp["xa"][:, 0:W], xin, AF.Abs)
    return s.sign(tmp["sgn"][:, 0:W], xin)


def qdq_vector(v, xin, out_bf, tmp, W, ve=None):
    nb = W // 16
    ins = v.reduce_max(tmp["rmax"][:, 0:nb].rearrange("p (b o) -> p b o", o=1),
                       xin.rearrange("p (b o) -> p b o", o=16),
                       mybir.AxisListType.X, apply_absolute_value=True)
    if ve is not None:
        ve.edge(v, ins)
    ins = v.tensor_scalar(tmp["scale"][:, 0:nb], tmp["rmax"][:, 0:nb],
                          1.0 / 6.0, 1e-12, ALU.mult, ALU.max)
    if ve is not None:
        ve.edge(v, ins)
    ins = v.reciprocal(tmp["inv"][:, 0:nb], tmp["scale"][:, 0:nb])
    if ve is not None:
        ve.edge(v, ins)
    inv_b = tmp["inv"][:, 0:nb][:, :, None].broadcast_to([128, nb, 16])
    t3 = tmp["t"][:, 0:W].rearrange("p (b o) -> p b o", o=16)
    v.tensor_tensor(t3, tmp["xa"][:, 0:W].rearrange("p (b o) -> p b o", o=16),
                    inv_b, ALU.mult)
    t = tmp["t"][:, 0:W]
    v.tensor_scalar(tmp["a1"][:, 0:W], t, 0.25, None, ALU.is_gt)
    for thr in (0.75, 1.25, 1.75):
        v.scalar_tensor_tensor(tmp["a1"][:, 0:W], t, thr, tmp["a1"][:, 0:W],
                               ALU.is_gt, ALU.add)
    v.tensor_scalar(tmp["a2"][:, 0:W], t, 2.5, None, ALU.is_gt)
    v.scalar_tensor_tensor(tmp["a2"][:, 0:W], t, 3.5, tmp["a2"][:, 0:W],
                           ALU.is_gt, ALU.add)
    v.tensor_scalar(tmp["a3"][:, 0:W], t, 5.0, None, ALU.is_gt)
    v.scalar_tensor_tensor(tmp["t"][:, 0:W], tmp["a1"][:, 0:W], 0.5,
                           tmp["a2"][:, 0:W], ALU.mult, ALU.add)
    v.scalar_tensor_tensor(tmp["t"][:, 0:W], tmp["a3"][:, 0:W], 2.0,
                           tmp["t"][:, 0:W], ALU.mult, ALU.add)
    v.tensor_tensor(tmp["t"][:, 0:W], tmp["t"][:, 0:W], tmp["sgn"][:, 0:W],
                    ALU.mult)
    sc_b = tmp["scale"][:, 0:nb][:, :, None].broadcast_to([128, nb, 16])
    return v.tensor_tensor(out_bf.rearrange("p (b o) -> p b o", o=16),
                           tmp["t"][:, 0:W].rearrange("p (b o) -> p b o", o=16),
                           sc_b, ALU.mult)


def emit_prep(ctx):
    nc = ctx.nc
    d = ctx.d
    sm = ctx.sems
    s_ld, s_tp, s_ev, s_st, s_q = (sm["ld"], sm["tp"], sm["ev"], sm["st"],
                                   sm["q"])
    s_ldg = sm["ldg"]
    s_q2 = sm["q2"]
    jobs = [
        ("L0_ffn_w1_w", DH1, D, "z_w1_blk", "blk", False),
        ("L0_ffn_w2_w", DH1, D, "z_w2_blk", "blk", False),
        ("L0_ffn_h1_w", DH1, DH1, "z_h1_blk", "blk", False),
        ("L0_ffn_h2_w", DH1, DH1, "z_h2_blk", "blk", False),
        ("L0_ffn_w3_w", D, DH1, "z_w3t", "flat", False),
        ("L1_ffn_w_gate", DH2, D, "z_wgt", "flat", True),
        ("L1_ffn_w_up", DH2, D, "z_wut", "flat", True),
        ("L1_ffn_w_down", D, DH2, "z_wdt", "flat", True),
    ]
    if PREP_JOBS is not None:
        jobs = [j for j in jobs if j[0] in PREP_JOBS]
    IMAX = DH1
    with (
        nc.sbuf_tensor("p_wld", [128, 2, IMAX], BF16) as wld,
        nc.sbuf_tensor("p_wldf", [128, 2, IMAX], F32) as wldf,
        nc.sbuf_tensor("p_asm", [128, 2, IMAX // 128, 128], BF16) as asm,
        nc.sbuf_tensor("p_xa", [128, IMAX], F32) as q_xa,
        nc.sbuf_tensor("p_sgn", [128, IMAX], F32) as q_sgn,
        nc.sbuf_tensor("p_t", [128, IMAX], F32) as q_t,
        nc.sbuf_tensor("p_a1", [128, IMAX], F32) as q_a1,
        nc.sbuf_tensor("p_a2", [128, IMAX], F32) as q_a2,
        nc.sbuf_tensor("p_a3", [128, IMAX], F32) as q_a3,
        nc.sbuf_tensor("p_rmax", [128, IMAX // 16], F32) as q_rmax,
        nc.sbuf_tensor("p_scale", [128, IMAX // 16], F32) as q_scale,
        nc.sbuf_tensor("p_inv", [128, IMAX // 16], F32) as q_inv,
        nc.psum_tensor("p_ps", [128, 4, 1024], BF16) as ps,
    ):
        identb = ctx.identb
        tmp = {"xa": q_xa, "sgn": q_sgn, "t": q_t, "a1": q_a1, "a2": q_a2,
               "a3": q_a3, "rmax": q_rmax, "scale": q_scale, "inv": q_inv}

        for (src, O, I, dst, dkind, qdq) in jobs:
            ntO, ntI = O // 128, I // 128
            spo = 1 if dkind == "blk" else ntI
            src_ap = d[src]
            dst_ap = d[dst]
            with nc.Block() as block:
                @block.gpsimd
                def _(g, src_ap=src_ap, ntO=ntO, ntI=ntI, I=I, qdq=qdq):
                    for ot in range(ntO):
                        if ot >= 2:
                            g.wait_ge(s_tp.h, s_tp.base + (ot - 1) * ntI)
                        if qdq:
                            break
                        g.dma_start(wld[:, ot % 2, 0:I],
                                    src_ap[ot * 128:(ot + 1) * 128, :]
                                    ).then_inc(s_ldg.h, 16)

                @block.scalar
                def _(s, ntO=ntO, I=I, qdq=qdq):
                    if not qdq:
                        return
                    for ot in range(ntO):
                        s.wait_ge(s_ld.h, s_ld.base + 16 * (ot + 1))
                        if ot >= 1:
                            s.wait_ge(s_q2.h, s_q2.base + ot)
                        qdq_stage1_s(s, wldf[:, ot % 2, 0:I], tmp, I)
                        s.sign(tmp["sgn"][:, I - 1:I],
                               wldf[:, ot % 2, I - 1:I]).then_inc(s_q.h, 1)

                @block.vector
                def _(v, ntO=ntO, ntI=ntI, I=I, qdq=qdq):
                    for ot in range(ntO):
                        if qdq:
                            v.wait_ge(s_ld.h, s_ld.base + 16 * (ot + 1))
                            v.wait_ge(s_q.h, s_q.base + ot + 1)
                            if ot >= 2:
                                v.wait_ge(s_tp.h, s_tp.base + (ot - 1) * ntI)
                            qdq_vector(v, wldf[:, ot % 2, 0:I],
                                       wld[:, ot % 2, 0:I], tmp, I)
                            v.engine_nop().then_inc(s_q2.h, 1)
                        for it in range(ntI):
                            k = ot * ntI + it
                            v.wait_ge(s_tp.h, s_tp.base + k + 1)
                            if ot >= 2 and it == 0:
                                v.wait_ge(s_st.h,
                                          s_st.base + 16 * spo * (ot - 1))
                            v.tensor_copy(asm[:, ot % 2, it, :],
                                          ps[:, k % 4, 0:128]).then_inc(s_ev.h, 1)

                @block.tensor
                def _(t, ntO=ntO, ntI=ntI, qdq=qdq):
                    t.wait_ge(s_ev.h, s_ev.base)
                    for ot in range(ntO):
                        if not qdq:
                            t.wait_ge(s_ldg.h,
                                      s_ldg.base + 16 * min(ot + 2, ntO))
                        else:
                            t.wait_ge(s_q2.h, s_q2.base + ot + 1)
                        for it in range(ntI):
                            k = ot * ntI + it
                            if k >= 4:
                                t.wait_ge(s_ev.h, s_ev.base + k - 3)
                            t.transpose(ps[:, k % 4, 0:128],
                                        wld[:, ot % 2, it * 128:(it + 1) * 128],
                                        identb[:]).then_inc(s_tp.h, 1)

                @block.sync
                def _(sync, ntO=ntO, ntI=ntI, src_ap=src_ap, dst_ap=dst_ap,
                      dkind=dkind, qdq=qdq, I=I):
                    def store(ot):
                        sync.wait_ge(s_ev.h, s_ev.base + (ot + 1) * ntI)
                        if dkind == "blk":
                            sync.dma_start(dst_ap[ot],
                                           asm[:, ot % 2, 0:ntI, :]
                                           ).then_inc(s_st.h, 16)
                        else:
                            for it in range(ntI):
                                sync.dma_start(
                                    dst_ap[it * 128:(it + 1) * 128,
                                           ot * 128:(ot + 1) * 128],
                                    asm[:, ot % 2, it, :]).then_inc(s_st.h, 16)
                    for ot in range(ntO):
                        if qdq:
                            if ot >= 2:
                                sync.wait_ge(s_q2.h, s_q2.base + ot - 1)
                            sync.dma_start(wldf[:, ot % 2, 0:I],
                                           src_ap[ot * 128:(ot + 1) * 128, :]
                                           ).then_inc(s_ld.h, 16)
                        if ot >= 1:
                            store(ot - 1)
                    store(ntO - 1)
            if qdq:
                s_ld.base += 16 * ntO
            else:
                s_ldg.base += 16 * ntO
            s_tp.base += ntO * ntI
            s_ev.base += ntO * ntI
            s_st.base += 16 * ntO * spo
            if qdq:
                s_q.base += ntO
                s_q2.base += ntO

        with nc.Block() as block:
            @block.sync
            def _(sync):
                sync.wait_ge(s_st.h, s_st.base)


def run_plan(nc, s_m, entries):
    entries = [e if len(e) == 4 else (e[0], e[1], e[2], e[1] != "sync")
               for e in entries]
    counted = [e for e in entries if e[3]]
    stages = sorted({e[0] for e in entries})
    nbefore = {}
    tot = 0
    for st in stages:
        nbefore[st] = tot
        tot += sum(1 for e in counted if e[0] == st)
    with nc.Block() as block:
        for engname in ("gpsimd", "vector", "scalar", "tensor", "sync"):
            sub = [e for e in entries if e[1] == engname]
            if not sub:
                continue

            def make(sub):
                def emit(eng):
                    last_wait = -1
                    for (st, _, fn, cnt) in sub:
                        w = s_m.base + nbefore[st]
                        if w > 0 and w != last_wait:
                            eng.wait_ge(s_m.h, w)
                            last_wait = w
                        ins = fn(eng)
                        if cnt:
                            ins.then_inc(s_m.h, 1)
                return emit

            getattr(block, engname)(make(sub))
    s_m.base += tot


def ln_tiles_entries(E, st, ctx, src, sq_dst, ast, eps_col, nt=4,
                     ldwait=None):
    AFl = AF
    ve = ctx.sems["ev2"]
    X = mybir.AxisListType.X

    def f_sq(s):
        if ldwait is not None:
            ldwait(s)
        for t in range(nt):
            ins = s.activation(sq_dst(t), src(t), AFl.Square)
        return ins
    E.append((st, "scalar", f_sq)); st += 1

    def f_red(v):
        if ldwait is not None:
            ldwait(v)
        for t in range(nt):
            v.reduce_sum(ast[:, t:t + 1], src(t), X)
            ins = v.reduce_sum(ast[:, 4 + t:5 + t], sq_dst(t), X)
        return ins
    E.append((st, "vector", f_red)); st += 1

    def f_stats(v):
        v.tensor_scalar(ast[:, 0:nt], ast[:, 0:nt], 1.0 / D, None, ALU.mult)
        ins = v.tensor_scalar(ast[:, 4:4 + nt], ast[:, 4:4 + nt], 1.0 / D,
                              None, ALU.mult)
        ve.edge(v, ins)
        ins = v.tensor_mul(ast[:, 28:28 + nt], ast[:, 0:nt], ast[:, 0:nt])
        ve.edge(v, ins)
        ins = v.tensor_sub(ast[:, 28:28 + nt], ast[:, 28:28 + nt],
                           ast[:, 4:4 + nt])
        ve.edge(v, ins)
        return v.tensor_scalar(ast[:, 12:12 + nt], ast[:, 28:28 + nt], -1.0,
                               eps_col, ALU.mult, ALU.add)
    E.append((st, "vector", f_stats)); st += 1

    def f_sqrt(s):
        return s.activation(ast[:, 16:16 + nt], ast[:, 12:12 + nt], AFl.Sqrt)
    E.append((st, "scalar", f_sqrt)); st += 1

    def f_newton(v):
        ins = v.reciprocal(ast[:, 20:20 + nt], ast[:, 16:16 + nt])
        ve.edge(v, ins)
        ins = v.tensor_mul(ast[:, 24:24 + nt], ast[:, 20:20 + nt],
                           ast[:, 20:20 + nt])
        ve.edge(v, ins)
        ins = v.tensor_mul(ast[:, 24:24 + nt], ast[:, 24:24 + nt],
                           ast[:, 12:12 + nt])
        ve.edge(v, ins)
        ins = v.tensor_scalar(ast[:, 24:24 + nt], ast[:, 24:24 + nt], -0.5,
                              1.5, ALU.mult, ALU.add)
        ve.edge(v, ins)
        return v.tensor_mul(ast[:, 8:8 + nt], ast[:, 20:20 + nt],
                            ast[:, 24:24 + nt])
    E.append((st, "vector", f_newton)); st += 1
    return st


ln_fd4_entries = None


def emit_phaseA(ctx):
    nc = ctx.nc
    d = ctx.d
    sm = ctx.sems
    s_m, s_ld, s_st = sm["m"], sm["ld"], sm["st"]
    s_ldg = sm["ldg"]
    ve = sm["ev2"]
    X = mybir.AxisListType.X
    identb = ctx.identb

    specs = [
        ("a_x", [128, 4, D], F32), ("a_xn1", [128, 4, D], F32),
        ("a_xx", [128, 4, D], F32), ("a_cv", [128, D], BF16),
        ("a_xt", [128, 8, GRP], BF16), ("a_xn1t", [128, 8, GRP], BF16),
        ("a_ot", [128, 8, GRP], BF16), ("a_lnc", [128, 8, GRP], BF16),
        ("a_u", [128, 8, GRP], F32), ("a_cx", [128, 8, GRP], F32),
        ("a_sq", [128, GRP], F32), ("a_carry", [128, 8], F32),
        ("a_stats", [128, 32], F32), ("a_rows", [1, 8, GRP], F32),
        ("a_wt", [128, 8, D], BF16), ("a_wo", [128, 8, D], BF16),
        ("a_wg", [128, 8, D], BF16), ("a_bc", [128, 5, D], F32),
        ("a_cols", [128, 6, 8], F32), ("a_rowld", [8, 2, 128], F32),
        ("a_identf", [128, 128], F32), ("a_onesr", [1, 128], F32),
        ("a_onesc", [128, 1], F32), ("a_wrow", [1, 1, D], F32),
        ("a_kc", [128, 2], F32),
    ]
    with contextlib.ExitStack() as stack:
        tl = {s_[0]: stack.enter_context(nc.sbuf_tensor(*s_)) for s_ in specs}
        ptp = stack.enter_context(nc.psum_tensor("a_ptp", [128, 4, 1024], BF16))
        pf = stack.enter_context(nc.psum_tensor("a_pf", [128, 4, 512], F32))
        (ax, axn1, axx, acv, axt, axn1t, aot, alnc, au, acx, acxsq, acarry,
         ast, arows, wt_sb, wo_sb, wg_sb, abc, acols, arowld, identf,
         onesr, onesc, awrow, akc) = (tl[s_[0]] for s_ in specs)

        with nc.Block() as block:
            @block.gpsimd
            def _(g):
                if s_m.base > 0:
                    g.wait_ge(s_m.h, s_m.base)
                if s_st.base > 0:
                    g.wait_ge(s_st.h, s_st.base)
                g.dma_start(wt_sb[:],
                            d["L0_attn_Wt"].rearrange("(ko ki) d -> ki ko d",
                                                      ki=128)).then_inc(s_ldg.h, 16)
                g.dma_start(wo_sb[:],
                            d["L0_attn_Wo"].rearrange("(ko ki) d -> ki ko d",
                                                      ki=128)).then_inc(s_ldg.h, 16)
                g.wait_ge(s_ldg.h, s_ldg.base + 32)

            @block.vector
            def _(v):
                if s_st.base > 0:
                    v.wait_ge(s_st.h, s_st.base)
                v.memset(akc[:, 0:1], EPS0)
                v.memset(akc[:, 1:2], EPS1)
        s_ldg.base += 32

        ROWS = ["L0_attn_ln1_w", "L0_attn_ln1_b", "L0_attn_lng_w",
                "L0_attn_lng_b", "L0_attn_bg"]
        E = []

        def f_l0(sync):
            sync.dma_start(onesr[:], d["c_ones_row"]).then_inc(s_ld.h, 16)
            sync.dma_start(awrow[0:1, 0, :], d[ROWS[0]][None, :]
                           ).then_inc(s_ld.h, 16)
            return None
        E.append((0, "sync", f_l0))
        st = 1
        for i in range(5):
            def f_bc(te, i=i):
                te.wait_ge(s_ld.h, s_ld.base + 16 * (2 + i))
                for ncc in range(2):
                    ins = te.matmul(pf[:, ncc, :], onesr[:],
                                    awrow[0:1, 0, ncc * 512:(ncc + 1) * 512],
                                    start=True, stop=True)
                return ins
            E.append((st, "tensor", f_bc)); st += 1

            def f_ev(v, i=i):
                return v.tensor_copy(abc[:, i, :],
                                     pf[:, 0:2, :].rearrange("p a b -> p (a b)"))
            E.append((st, "vector", f_ev))
            if i < 4:
                def f_li(sync, i=i):
                    sync.dma_start(awrow[0:1, 0, :], d[ROWS[i + 1]][None, :]
                                   ).then_inc(s_ld.h, 16)
                    return None
                E.append((st, "sync", f_li))
            else:
                def f_lm(sync):
                    sync.dma_start(arowld[0:8, 0, :],
                                   d["L0_attn_lnc_w"].rearrange("(a b) -> a b",
                                                                b=128)
                                   ).then_inc(s_ld.h, 16)
                    sync.dma_start(arowld[0:8, 1, :],
                                   d["L0_attn_lnc_b"].rearrange("(a b) -> a b",
                                                                b=128)
                                   ).then_inc(s_ld.h, 16)
                    sync.dma_start(identf[:], d["c_identf"]).then_inc(s_ld.h, 16)
                    sync.dma_start(onesc[:], d["c_ones_col"]).then_inc(s_ld.h, 16)
                    sync.dma_start(acols[:, 4, :], d["c_lam"]).then_inc(s_ld.h, 16)
                    sync.dma_start(acols[:, 5, :], d["c_beta"]
                                   ).then_inc(s_ld.h, 16)
                    return None
                E.append((st, "sync", f_lm))
            st += 1

        def f_tc(te):
            te.wait_ge(s_ld.h, s_ld.base + 16 * 12)
            te.transpose(pf[0:128, 2, 0:8], arowld[0:8, 0, :], identf[0:8, 0:8])
            return te.transpose(pf[0:128, 3, 0:8], arowld[0:8, 1, :],
                                identf[0:8, 0:8])
        E.append((st, "tensor", f_tc)); st += 1

        def f_tcev(v):
            v.tensor_copy(acols[:, 0, :], pf[:, 2, 0:8])
            return v.tensor_copy(acols[:, 1, :], pf[:, 3, 0:8])
        E.append((st, "vector", f_tcev)); st += 1
        run_plan(nc, s_m, E)
        s_ld.base += 16 * 12

        dbg_extra = (23 * 16) if DBG_PA else 0
        ldg = 4 * 16
        ldgg = 2 * 16
        stg = 12 * 16
        for g in range(NGRP if PA_GROUPS is None else PA_GROUPS):
            b, seg = divmod(g, NGRP // BLOC)
            r0 = g * GRP
            E = []
            st = 0

            def f_ld1(sync, g=g, r0=r0):
                for t in range(4):
                    sync.dma_start(ax[:, t, :],
                                   d["x"][r0 + t * 128:r0 + (t + 1) * 128, :]
                                   ).then_inc(s_ld.h, 16)
                return None
            E.append((st, "sync", f_ld1))

            def f_ldw0(g_, g=g):
                g_.dma_start(
                    wg_sb[:],
                    d["L0_attn_Wg"].rearrange("(hh ko ki) d -> hh ki ko d",
                                              hh=2, ki=128)[0]
                    ).then_inc(s_ldg.h, 16)
                return None
            E.append((st, "gpsimd", f_ldw0, False))

            def mkwait(g=g):
                def w(eng):
                    eng.wait_ge(s_ld.h, s_ld.base + ldg * g + 4 * 16)
                return w
            st = ln_tiles_entries(E, st, ctx, lambda t: ax[:, t, :],
                                  lambda t: axn1[:, t, :], ast,
                                  akc[:, 0:1], ldwait=mkwait())

            def f_ap0(v):
                for t in range(4):
                    ins = v.tensor_scalar(axn1[:, t, :], ax[:, t, :],
                                          ast[:, t:t + 1], ast[:, 8 + t:9 + t],
                                          ALU.subtract, ALU.mult)
                return ins
            E.append((st, "vector", f_ap0)); st += 1

            st = ln_tiles_entries(E, st, ctx, lambda t: axn1[:, t, :],
                                  lambda t: ax[:, t, :], ast, akc[:, 1:2])

            def f_ap1(v):
                for t in range(4):
                    v.tensor_scalar(axx[:, t, :], axn1[:, t, :],
                                    ast[:, t:t + 1], ast[:, 8 + t:9 + t],
                                    ALU.subtract, ALU.mult)
                for t in range(4):
                    v.tensor_mul(axx[:, t, :], axx[:, t, :], abc[:, 0, :])
                for t in range(4):
                    ins = v.tensor_add(axx[:, t, :], axx[:, t, :], abc[:, 1, :])
                return ins
            E.append((st, "vector", f_ap1)); st += 1

            for (srcbuf, dstT, tag) in ((axn1, axn1t, "xn1"), (axx, axt, "xx")):
                for t in range(4):
                    def f_cv(s, srcbuf=srcbuf, t=t, g=g, tag=tag):
                        if tag == "xn1" and t == 0 and g >= 1:
                            s.wait_ge(s_st.h, s_st.base + stg * (g - 1) + 8 * 16
                                      + (dbg_extra if g >= 2 else 0))
                        return s.copy(acv[:], srcbuf[:, t, :])
                    E.append((st, "scalar", f_cv)); st += 1
                    for half in range(2):
                        def f_tp(te, t=t, half=half):
                            for k in range(4):
                                ins = te.transpose(
                                    ptp[:, k, 0:128],
                                    acv[:, (half * 4 + k) * 128:
                                        (half * 4 + k + 1) * 128],
                                    identb[:])
                            return ins
                        E.append((st, "tensor", f_tp)); st += 1

                        def f_te(v, dstT=dstT, t=t, half=half):
                            for k in range(4):
                                dt = half * 4 + k
                                ins = v.tensor_copy(
                                    dstT[:, dt, t * 128:(t + 1) * 128],
                                    ptp[:, k, 0:128])
                            return ins
                        E.append((st, "vector", f_te)); st += 1

            def f_stx(sync, g=g, r0=r0):
                for dt in range(8):
                    sync.dma_start(d["z_xn1t"][dt][:, r0:r0 + GRP],
                                   axn1t[:, dt, :]).then_inc(s_st.h, 16)
                return None
            E.append((st, "sync", f_stx))

            for dt in range(8):
                def f_u(te, dt=dt):
                    for kt in range(8):
                        ins = te.matmul(pf[:, dt % 2, :],
                                        wt_sb[:, kt, dt * 128:(dt + 1) * 128],
                                        axt[:, kt, :], start=(kt == 0),
                                        stop=(kt == 7))
                    return ins
                E.append((st, "tensor", f_u)); st += 1

                def f_ue(v, dt=dt):
                    return v.tensor_scalar(au[:, dt, :], pf[:, dt % 2, :],
                                           acols[:, 5, dt:dt + 1], None,
                                           ALU.mult)
                E.append((st, "vector", f_ue)); st += 1

            for dt in range(8):
                def f_scan(v, dt=dt, seg=seg):
                    init = 0.0 if seg == 0 else acarry[:, dt:dt + 1]
                    return v.tensor_tensor_scan(
                        acx[:, dt, :],
                        acols[:, 4, dt:dt + 1].broadcast_to([128, GRP]),
                        au[:, dt, :], init, ALU.mult, ALU.add)
                E.append((st, "vector", f_scan)); st += 1

                def f_cs(v, dt=dt):
                    return v.tensor_copy(acarry[:, dt:dt + 1],
                                         acx[:, dt, GRP - 1:GRP])
                E.append((st, "vector", f_cs)); st += 1

            for dt in range(8):
                def f_sq2(s, dt=dt):
                    return s.activation(acxsq[:], acx[:, dt, :], AF.Square)
                E.append((st, "scalar", f_sq2)); st += 1

                def f_st2(te, dt=dt):
                    te.matmul(pf[0:1, 2, :], onesc[:], acx[:, dt, :],
                              start=(dt == 0), stop=(dt == 7))
                    return te.matmul(pf[0:1, 3, :], onesc[:], acxsq[:],
                                     start=(dt == 0), stop=(dt == 7))
                E.append((st, "tensor", f_st2)); st += 1

            def f_rows(v):
                v.tensor_scalar(arows[0:1, 0, :], pf[0:1, 2, :], 1.0 / D, None,
                                ALU.mult)
                ins = v.tensor_scalar(arows[0:1, 1, :], pf[0:1, 3, :], 1.0 / D,
                                      None, ALU.mult)
                ve.edge(v, ins)
                ins = v.tensor_mul(arows[0:1, 6, :], arows[0:1, 0, :],
                                   arows[0:1, 0, :])
                ve.edge(v, ins)
                ins = v.tensor_sub(arows[0:1, 6, :], arows[0:1, 6, :],
                                   arows[0:1, 1, :])
                ve.edge(v, ins)
                return v.tensor_scalar(arows[0:1, 3, :], arows[0:1, 6, :],
                                       -1.0, akc[0:1, 1:2], ALU.mult, ALU.add)
            E.append((st, "vector", f_rows)); st += 1

            def f_rsq(s):
                return s.activation(arows[0:1, 4, :], arows[0:1, 3, :], AF.Sqrt)
            E.append((st, "scalar", f_rsq)); st += 1

            def f_rnew(v):
                ins = v.reciprocal(arows[0:1, 5, :], arows[0:1, 4, :])
                ve.edge(v, ins)
                ins = v.tensor_mul(arows[0:1, 6, :], arows[0:1, 5, :],
                                   arows[0:1, 5, :])
                ve.edge(v, ins)
                ins = v.tensor_mul(arows[0:1, 6, :], arows[0:1, 6, :],
                                   arows[0:1, 3, :])
                ve.edge(v, ins)
                ins = v.tensor_scalar(arows[0:1, 6, :], arows[0:1, 6, :],
                                      -0.5, 1.5, ALU.mult, ALU.add)
                ve.edge(v, ins)
                ins = v.tensor_mul(arows[0:1, 2, :], arows[0:1, 5, :],
                                   arows[0:1, 6, :])
                ve.edge(v, ins)
                return v.tensor_mul(arows[0:1, 7, :], arows[0:1, 0, :],
                                    arows[0:1, 2, :])
            E.append((st, "vector", f_rnew)); st += 1

            def f_bc2(te):
                te.matmul(pf[0:128, 2, :], onesr[:], arows[0:1, 2, :],
                          start=True, stop=True)
                return te.matmul(pf[0:128, 3, :], onesr[:], arows[0:1, 7, :],
                                 start=True, stop=True)
            E.append((st, "tensor", f_bc2)); st += 1

            def f_lnc(v):
                for dt in range(8):
                    v.tensor_mul(au[:, dt, :], acx[:, dt, :], pf[:, 2, :])
                    v.tensor_sub(au[:, dt, :], au[:, dt, :], pf[:, 3, :])
                    ins = v.tensor_scalar(alnc[:, dt, :], au[:, dt, :],
                                          acols[:, 0, dt:dt + 1],
                                          acols[:, 1, dt:dt + 1],
                                          ALU.mult, ALU.add)
                return ins
            E.append((st, "vector", f_lnc)); st += 1

            for t in range(4):
                def f_wo(te, t=t):
                    for kt in range(8):
                        for ncc in range(2):
                            ins = te.matmul(pf[:, ncc, :],
                                            alnc[:, kt, t * 128:(t + 1) * 128],
                                            wo_sb[:, kt,
                                                  ncc * 512:(ncc + 1) * 512],
                                            start=(kt == 0), stop=(kt == 7))
                    return ins
                E.append((st, "tensor", f_wo)); st += 1

                def f_woe(v, t=t):
                    return v.tensor_copy(
                        ax[:, t, :], pf[:, 0:2, :].rearrange("p a b -> p (a b)"))
                E.append((st, "vector", f_woe)); st += 1

                def f_ocv(s, t=t):
                    return s.copy(acv[:], ax[:, t, :])
                E.append((st, "scalar", f_ocv)); st += 1
                for half in range(2):
                    def f_otp(te, t=t, half=half):
                        for k in range(4):
                            ins = te.transpose(
                                ptp[:, k, 0:128],
                                acv[:, (half * 4 + k) * 128:
                                    (half * 4 + k + 1) * 128],
                                identb[:])
                        return ins
                    E.append((st, "tensor", f_otp)); st += 1

                    def f_ote(v, t=t, half=half):
                        for k in range(4):
                            dt = half * 4 + k
                            ins = v.tensor_copy(
                                aot[:, dt, t * 128:(t + 1) * 128],
                                ptp[:, k, 0:128])
                        return ins
                    E.append((st, "vector", f_ote)); st += 1

            for t in range(4):
                def f_gm0(te, t=t, g=g):
                    if t == 0:
                        te.wait_ge(s_ldg.h, s_ldg.base + ldgg * g + 16)
                    for kt in range(8):
                        for ncc in range(2):
                            ins = te.matmul(pf[:, 2 + ncc, :],
                                            axt[:, kt, t * 128:(t + 1) * 128],
                                            wg_sb[:, kt, ncc * 512:
                                                  (ncc + 1) * 512],
                                            start=(kt == 0), stop=(kt == 7))
                    return ins
                E.append((st, "tensor", f_gm0)); st += 1

                def f_ge0(v, t=t):
                    return v.tensor_copy(
                        axn1[:, t, :],
                        pf[:, 2:4, :].rearrange("p a b -> p (a b)"))
                E.append((st, "vector", f_ge0)); st += 1

            def f_ld2(g_, g=g):
                g_.dma_start(
                    wg_sb[:],
                    d["L0_attn_Wg"].rearrange("(hh ko ki) d -> hh ki ko d",
                                              hh=2, ki=128)[1]
                    ).then_inc(s_ldg.h, 16)
                return None
            E.append((st, "gpsimd", f_ld2, False))

            for t in range(4):
                def f_gm1(te, t=t, g=g):
                    if t == 0:
                        te.wait_ge(s_ldg.h, s_ldg.base + ldgg * g + 32)
                    for kt in range(8):
                        for ncc in range(2):
                            ins = te.matmul(pf[:, 2 + ncc, :],
                                            aot[:, kt, t * 128:(t + 1) * 128],
                                            wg_sb[:, kt, ncc * 512:
                                                  (ncc + 1) * 512],
                                            start=(kt == 0), stop=(kt == 7))
                    return ins
                E.append((st, "tensor", f_gm1)); st += 1

                def f_ge1(v, t=t):
                    v.tensor_add(axn1[:, t, :], axn1[:, t, :],
                                 pf[:, 2:4, :].rearrange("p a b -> p (a b)"))
                    return v.tensor_add(axn1[:, t, :], axn1[:, t, :],
                                        abc[:, 4, :])
                E.append((st, "vector", f_ge1)); st += 1

            st = ln_tiles_entries(
                E, st, ctx, lambda t: axn1[:, t, :],
                lambda t: acx[:, 2 * t:2 * t + 2, :].rearrange(
                    "p a b -> p (a b)"),
                ast, akc[:, 1:2])

            def f_gap(v):
                for t in range(4):
                    v.tensor_scalar(axn1[:, t, :], axn1[:, t, :],
                                    ast[:, t:t + 1], ast[:, 8 + t:9 + t],
                                    ALU.subtract, ALU.mult)
                for t in range(4):
                    v.tensor_mul(axn1[:, t, :], axn1[:, t, :], abc[:, 2, :])
                for t in range(4):
                    ins = v.tensor_add(axn1[:, t, :], axn1[:, t, :],
                                       abc[:, 3, :])
                return ins
            E.append((st, "vector", f_gap)); st += 1

            def f_g5(s):
                for t in range(4):
                    ins = s.activation(axn1[:, t, :], axn1[:, t, :], AF.Sigmoid)
                return ins
            E.append((st, "scalar", f_g5)); st += 1

            def f_c1(v):
                for t in range(4):
                    v.tensor_sub(ax[:, t, :], ax[:, t, :], axx[:, t, :])
                for t in range(4):
                    v.tensor_mul(ax[:, t, :], ax[:, t, :], axn1[:, t, :])
                for t in range(4):
                    ins = v.tensor_add(ax[:, t, :], ax[:, t, :], axx[:, t, :])
                return ins
            E.append((st, "vector", f_c1)); st += 1

            def f_sts(sync, r0=r0):
                for t in range(4):
                    sync.dma_start(d["z_sn"][r0 + t * 128:r0 + (t + 1) * 128, :],
                                   ax[:, t, :]).then_inc(s_st.h, 16)
                return None
            E.append((st, "sync", f_sts))

            if DBG_PA and g == 0:
                pass
            run_plan(nc, s_m, E)

        ngr = NGRP if PA_GROUPS is None else PA_GROUPS
        s_ld.base += ldg * ngr
        s_ldg.base += ldgg * ngr
        s_st.base += stg * ngr
        with nc.Block() as block:
            @block.sync
            def _(sync):
                sync.wait_ge(s_st.h, s_st.base)


TB = 512
NBLK = S // TB


def emit_phaseB(ctx):
    nc = ctx.nc
    d = ctx.d
    sm = ctx.sems
    s_m, s_ld, s_ldg, s_st = sm["m"], sm["ld"], sm["ldg"], sm["st"]
    identb = ctx.identb

    specs = [
        ("b_xn1t", [128, 8, TB], BF16),
        ("b_at", [128, 32, TB], BF16),
        ("b_bt", [128, 32, TB], BF16),
        ("b_ws", [128, 4, 8, 128], BF16),
        ("b_wh", [128, 3, 32, 128], BF16),
        ("b_w3", [128, 4, 1024], BF16),
        ("b_tmp", [128, 2, 2, 512], F32),
        ("b_gx", [128, 2, D], F32),
        ("b_x2", [128, 2, D], F32),
        ("b_xs", [128, 2, 2, D], F32),
        ("b_cv", [128, D], BF16),
        ("b_xt2", [128, 8, TB], BF16),
        ("b_cols", [128, 4, 32], F32),
        ("b_w3bc", [128, D], F32),
        ("b_rowld", [32, 4, 128], F32),
        ("b_st", [128, 32], F32),
        ("b_kc", [128, 2], F32),
        ("b_identf", [128, 128], F32),
        ("b_onesr", [1, 128], F32),
        ("b_wrow", [1, 1, D], F32),
    ]
    with contextlib.ExitStack() as stack:
        tl = {s_[0]: stack.enter_context(nc.sbuf_tensor(*s_)) for s_ in specs}
        pfA = stack.enter_context(nc.psum_tensor("b_pfA", [128, 4, 512], F32))
        pfB = stack.enter_context(nc.psum_tensor("b_pfB", [128, 2, 512], F32))
        ptp = stack.enter_context(nc.psum_tensor("b_ptp", [128, 2, 1024], BF16))
        (xn1t, at_, bt_, ws, wh, w3s, btmp, bgx, bx2, bxs, bcv, bxt2, bcols,
         w3bc, browld, bst, bkc, identf, onesr, bwrow) = (tl[s_[0]]
                                                          for s_ in specs)

        with nc.Block() as block:
            @block.sync
            def _(sync):
                for i, nm in enumerate(["L0_ffn_w1_s", "L0_ffn_w2_s",
                                        "L0_ffn_h1_s", "L0_ffn_h2_s"]):
                    sync.dma_start(browld[:, i, :],
                                   d[nm].rearrange("(a b) -> a b", b=128)
                                   ).then_inc(s_ld.h, 16)
                sync.dma_start(identf[:], d["c_identf"]).then_inc(s_ld.h, 16)
                sync.dma_start(onesr[:], d["c_ones_row"]).then_inc(s_ld.h, 16)
                sync.dma_start(bwrow[0:1, 0, :], d["L0_ffn_w3_s"][None, :]
                               ).then_inc(s_ld.h, 16)
                sync.wait_ge(s_ld.h, s_ld.base + 7 * 16)

            @block.vector
            def _(v):
                v.wait_ge(s_m.h, s_m.base)
                v.wait_ge(s_st.h, s_st.base)
                v.memset(bkc[:, 0:1], EPS0)
                v.memset(bkc[:, 1:2], EPS1)
        s_ld.base += 7 * 16

        E = []
        st = 0

        def f_t4(te):
            te.wait_ge(s_ld.h, s_ld.base)
            for i in range(4):
                ins = te.transpose(pfA[0:128, i, 0:32], browld[0:32, i, :],
                                   identf[0:32, 0:32])
            return ins
        E.append((st, "tensor", f_t4)); st += 1

        def f_t4e(v):
            for i in range(4):
                ins = v.tensor_copy(bcols[:, i, :], pfA[:, i, 0:32])
            return ins
        E.append((st, "vector", f_t4e)); st += 1

        def f_w3bc(te):
            for ncc in range(2):
                ins = te.matmul(pfA[:, ncc, :], onesr[:],
                                bwrow[0:1, 0, ncc * 512:(ncc + 1) * 512],
                                start=True, stop=True)
            return ins
        E.append((st, "tensor", f_w3bc)); st += 1

        def f_w3bce(v):
            return v.tensor_copy(w3bc[:],
                                 pfA[:, 0:2, :].rearrange("p a b -> p (a b)"))
        E.append((st, "vector", f_w3bce)); st += 1
        run_plan(nc, s_m, E)

        ld_syncb = 16 * 16
        ld_gb = 192 * 16
        st_b = 16 * 16
        for blk in range(NBLK if PB_BLOCKS is None else PB_BLOCKS):
            r0 = blk * TB
            E = []
            st = 0
            lb = s_ld.base + ld_syncb * blk
            gb = s_ldg.base + ld_gb * blk
            sb = s_st.base + st_b * blk

            def f_lx(sync, r0=r0, blk=blk):
                for dt in range(8):
                    sync.dma_start(xn1t[:, dt, :],
                                   d["z_xn1t"][dt][:, r0:r0 + TB]
                                   ).then_inc(s_ld.h, 16)
                for w in range(2):
                    for i in range(2):
                        rt = r0 + (2 * w + i) * 128
                        sync.dma_start(bxs[:, w, 0, :] if i == 0 and False
                                       else bxs[:, i, 0, :] if False else
                                       bxs[:, i, 0, :],
                                       d["x"][rt:rt + 128, :]
                                       ).then_inc(s_ld.h, 16)
                return None

            def f_lx2(sync, r0=r0):
                for dt in range(8):
                    sync.dma_start(xn1t[:, dt, :],
                                   d["z_xn1t"][dt][:, r0:r0 + TB]
                                   ).then_inc(s_ld.h, 16)
                return None
            E.append((st, "sync", f_lx2))

            def mk_wl(dt, which):
                def f(g_):
                    g_.dma_start(ws[:, (2 * dt + which) % 4, :, :],
                                 d["z_w1_blk" if which == 0 else "z_w2_blk"][dt]
                                 ).then_inc(s_ldg.h, 16)
                    return None
                return f
            E.append((st, "gpsimd", mk_wl(0, 0), False))
            E.append((st, "gpsimd", mk_wl(0, 1), False))

            for dt in range(32):
                if dt + 1 < 32:
                    E.append((st, "gpsimd", mk_wl(dt + 1, 0), False))
                    E.append((st, "gpsimd", mk_wl(dt + 1, 1), False))

                def f_mm(te, dt=dt, gb=gb, lb=lb):
                    te.wait_ge(s_ldg.h, gb + 32 * min(dt + 2, 32))
                    if dt == 0:
                        te.wait_ge(s_ld.h, lb + 8 * 16)
                    for w_i in range(2):
                        for kt in range(8):
                            ins = te.matmul(pfA[:, 2 * (dt % 2) + w_i, :],
                                            ws[:, (2 * dt + w_i) % 4, kt, :],
                                            xn1t[:, kt, :], start=(kt == 0),
                                            stop=(kt == 7))
                    return ins
                E.append((st, "tensor", f_mm)); st += 1

                def f_ev(s, dt=dt):
                    s.activation(btmp[:, dt % 2, 0, :],
                                 pfA[:, 2 * (dt % 2), :], AF.Silu,
                                 scale=bcols[:, 0, dt:dt + 1])
                    return s.activation(btmp[:, dt % 2, 1, :],
                                        pfA[:, 2 * (dt % 2) + 1, :], AF.Copy,
                                        scale=bcols[:, 1, dt:dt + 1])
                E.append((st, "scalar", f_ev))

                def f_mu(v, dt=dt, blk=blk):
                    if dt == 0 and blk >= 1:
                        v.wait_ge(s_st.h, s_st.base + st_b * blk)
                    return v.tensor_mul(at_[:, dt, :], btmp[:, dt % 2, 0, :],
                                        btmp[:, dt % 2, 1, :])
                E.append((st + 1, "vector", f_mu))
                st += 1
            st += 1

            for (lay, srcb, dstb, scol, woff) in (
                    ("z_h1_blk", at_, bt_, 2, 64), ("z_h2_blk", bt_, at_, 3, 96)):
                def mk_hl(dt, lay=lay):
                    def f(g_):
                        g_.dma_start(wh[:, dt % 3, :, :], d[lay][dt]
                                     ).then_inc(s_ldg.h, 16)
                        return None
                    return f
                E.append((st, "gpsimd", mk_hl(0), False))
                E.append((st, "gpsimd", mk_hl(1), False))
                for dt in range(32):
                    if dt + 2 < 32:
                        E.append((st, "gpsimd", mk_hl(dt + 2), False))

                    def f_hmm(te, dt=dt, gb=gb, woff=woff, srcb=srcb):
                        te.wait_ge(s_ldg.h,
                                   gb + 16 * (woff + min(dt + 3, 32)))
                        for kt in range(32):
                            ins = te.matmul(pfA[:, dt % 4, :],
                                            wh[:, dt % 3, kt, :],
                                            srcb[:, kt, :], start=(kt == 0),
                                            stop=(kt == 31))
                        return ins
                    E.append((st, "tensor", f_hmm))

                    if dt >= 2:
                        def f_gel(s, dt=dt - 2, dstb=dstb, scol=scol):
                            return s.activation(dstb[:, dt, :],
                                                pfA[:, dt % 4, :], AF.Gelu,
                                                scale=bcols[:, scol, dt:dt + 1])
                        E.append((st, "scalar", f_gel))
                    st += 1
                for dt in (30, 31):
                    def f_gel2(s, dt=dt, dstb=dstb, scol=scol):
                        return s.activation(dstb[:, dt, :], pfA[:, dt % 4, :],
                                            AF.Gelu,
                                            scale=bcols[:, scol, dt:dt + 1])
                    E.append((st, "scalar", f_gel2)); st += 1

            for wave in range(2):
                def mk_w3l(kt, wave=wave):
                    def f(g_):
                        g_.dma_start(w3s[:, kt % 4, :],
                                     d["z_w3t"][kt * 128:(kt + 1) * 128, :]
                                     ).then_inc(s_ldg.h, 16)
                        return None
                    return f
                for kt in range(3):
                    E.append((st, "gpsimd", mk_w3l(kt), False))

                def f_lxs(sync, wave=wave, r0=r0):
                    for i in range(2):
                        rt = r0 + (2 * wave + i) * 128
                        sync.dma_start(bxs[:, i, 0, :], d["x"][rt:rt + 128, :]
                                       ).then_inc(s_ld.h, 16)
                        sync.dma_start(bxs[:, i, 1, :],
                                       d["z_sn"][rt:rt + 128, :]
                                       ).then_inc(s_ld.h, 16)
                    return None
                E.append((st, "sync", f_lxs))

                for kt in range(32):
                    if kt + 3 < 32:
                        E.append((st, "gpsimd", mk_w3l(kt + 3), False))

                    def f_w3m(te, kt=kt, wave=wave, gb=gb):
                        te.wait_ge(s_ldg.h,
                                   gb + 16 * (128 + wave * 32 +
                                              min(kt + 4, 32)))
                        for i in range(2):
                            tt = 2 * wave + i
                            for ncc in range(2):
                                ins = te.matmul(
                                    pfA[:, 2 * i + ncc, :],
                                    at_[:, kt, tt * 128:(tt + 1) * 128],
                                    w3s[:, kt % 4, ncc * 512:(ncc + 1) * 512],
                                    start=(kt == 0), stop=(kt == 31))
                        return ins
                    E.append((st, "tensor", f_w3m)); st += 1

                def f_gw(v, wave=wave, lb=lb):
                    v.wait_ge(s_ld.h, lb + 16 * (8 + 4 * (wave + 1)))
                    for i in range(2):
                        ins = v.tensor_mul(
                            bgx[:, i, :],
                            pfA[:, 2 * i:2 * i + 2, :].rearrange(
                                "p a b -> p (a b)"),
                            w3bc[:])
                    return ins
                E.append((st, "vector", f_gw)); st += 1

                def f_sil(s):
                    for i in range(2):
                        ins = s.activation(bx2[:, i, :], bgx[:, i, :], AF.Silu)
                    return ins
                E.append((st, "scalar", f_sil)); st += 1

                def f_cmb(v):
                    for i in range(2):
                        v.tensor_mul(bx2[:, i, :], bx2[:, i, :],
                                     bxs[:, i, 1, :])
                    for i in range(2):
                        ins = v.tensor_add(bx2[:, i, :], bx2[:, i, :],
                                           bxs[:, i, 0, :])
                    return ins
                E.append((st, "vector", f_cmb)); st += 1

                def f_stx2(sync, wave=wave, r0=r0):
                    for i in range(2):
                        rt = r0 + (2 * wave + i) * 128
                        sync.dma_start(d["z_x2n"][rt:rt + 128, :],
                                       bx2[:, i, :]).then_inc(s_st.h, 16)
                    return None
                E.append((st, "sync", f_stx2))

                st = ln_tiles_entries(E, st, ctx, lambda t: bx2[:, t, :],
                                      lambda t: bgx[:, t, :], bst,
                                      bkc[:, 0:1], nt=2)

                def f_ap2(v):
                    for i in range(2):
                        ins = v.tensor_scalar(bgx[:, i, :], bx2[:, i, :],
                                              bst[:, i:i + 1],
                                              bst[:, 8 + i:9 + i],
                                              ALU.subtract, ALU.mult)
                    return ins
                E.append((st, "vector", f_ap2)); st += 1

                def f_stxn(sync, wave=wave, r0=r0):
                    for i in range(2):
                        rt = r0 + (2 * wave + i) * 128
                        sync.dma_start(d["z_xn2n"][rt:rt + 128, :],
                                       bgx[:, i, :]).then_inc(s_st.h, 16)
                    return None
                E.append((st, "sync", f_stxn))

                for i in range(2):
                    def f_cv2(s, i=i):
                        return s.copy(bcv[:], bgx[:, i, :])
                    E.append((st, "scalar", f_cv2)); st += 1

                    def f_tp2(te, i=i):
                        for k in range(8):
                            ins = te.transpose(
                                ptp[:, k % 2, (k // 2) * 256 + (k % 2) * 0:
                                    0] if False else
                                ptp[:, k % 2, 128 * (k // 2):
                                    128 * (k // 2) + 128],
                                bcv[:, k * 128:(k + 1) * 128], identb[:])
                        return ins
                    E.append((st, "tensor", f_tp2)); st += 1

                    def f_te2(v, i=i, wave=wave):
                        tt = 2 * wave + i
                        for k in range(8):
                            ins = v.tensor_copy(
                                bxt2[:, k, tt * 128:(tt + 1) * 128],
                                ptp[:, k % 2, 128 * (k // 2):
                                    128 * (k // 2) + 128])
                        return ins
                    E.append((st, "vector", f_te2)); st += 1

            def f_stt(sync, r0=r0):
                for dt in range(8):
                    sync.dma_start(d["z_xn2t"][dt][:, r0:r0 + TB],
                                   bxt2[:, dt, :]).then_inc(s_st.h, 16)
                return None
            E.append((st, "sync", f_stt))
            run_plan(nc, s_m, E)

        nbl = NBLK if PB_BLOCKS is None else PB_BLOCKS
        s_ld.base += ld_syncb * nbl
        s_ldg.base += ld_gb * nbl
        s_st.base += st_b * nbl
        with nc.Block() as block:
            @block.sync
            def _(sync):
                sync.wait_ge(s_st.h, s_st.base)


def mm_group(te, mms):
    ins = None
    i = 0
    n = len(mms)
    while i < n:
        j = i
        while j + 1 < n and mms[j + 1][4] == mms[i][4]:
            j += 1
        for t in range(i, j + 1):
            out, lh, rh, tp, k = mms[t]
            ins = te.matmul(out, lh, rh, start=(t == i), stop=(t == j),
                            tile_position=tp, skip_group_check=True)
        i = j + 1
    return ins


def emit_phaseC(ctx):
    nc = ctx.nc
    d = ctx.d
    sm = ctx.sems
    s_m, s_ld, s_ldg, s_st = sm["m"], sm["ld"], sm["ldg"], sm["st"]
    ve = sm["ev2"]
    identb = ctx.identb
    NG = NGRP if PC_GROUPS is None else PC_GROUPS

    specs = [
        ("c_qkv", [128, 8, 4, GRP], BF16),
        ("c_xt", [128, 8, GRP], BF16),
        ("c_kr", [128, 2, 8, 4, 64], BF16),
        ("c_vr", [128, 2, 8, 4, 64], BF16),
        ("c_P6", [128, 16, 6, 64], BF16),
        ("c_Np", [128, 16, 2, 64], BF16),
        ("c_R", [128, 16, 64], BF16),
        ("c_Lt", [128, 16, 64], BF16),
        ("c_S", [128, 8, 64], F32),
        ("c_Sb", [128, 8, 64], BF16),
        ("c_yt", [128, 8, GRP], BF16),
        ("c_yn", [128, 4, 8, 2, 64], BF16),
        ("cs_s2", [128, 4, D], F32),
        ("c_kt4", [128, 2, GRP], F32),
        ("c_ksq4", [128, 2, GRP], F32),
        ("c_rn4", [2, 3, 2 * GRP], F32),
        ("c_wq", [128, 8, D], BF16), ("c_wk", [128, 8, D], BF16),
        ("c_wv", [128, 8, D], BF16), ("c_wo", [128, 8, D], BF16),
        ("c_ab", [128, 2, 8], F32),
        ("c_abr", [1, 2, 16], F32),
        ("c_rep", [16, 2, 8], F32),
        ("cs_tril", [128, 64], F32), ("cs_triu", [128, 64], F32),
        ("cs_h2", [128, 2], F32), ("cs_s2sel", [2, 128], F32),
        ("cs_S1", [16, 128], F32), ("cs_M16", [16, 8], F32),
        ("cs_on8", [1, 8], F32),
    ]
    with contextlib.ExitStack() as stack:
        tl = {s_[0]: stack.enter_context(nc.sbuf_tensor(*s_)) for s_ in specs}
        pC = stack.enter_context(nc.psum_tensor("c_pC", [128, 6, 512], F32))
        pT = stack.enter_context(nc.psum_tensor("c_pT", [128, 2, 1024], BF16))
        (qkv, cxt, krows, vrows, cP6, cNp, cR, cLt, cS, cSb, cyt, cyN, cs2,
         ckt4, cksq4, crn4, wq_sb, wk_sb, wv_sb, wo_sb, cab, cabr, crep,
         ctril, ctriu, ch2, cs2sel, cS1, cM16, con8) = (tl[s_[0]]
                                                        for s_ in specs)
        vKK = pC[:, 0:2, :].rearrange("p a (h c) -> p (a h) c", c=64)
        vKQ = pC[:, 2:4, :].rearrange("p a (h c) -> p (a h) c", c=64)
        vN2 = pC[:, 0:2, :].rearrange("p a (h c) -> p (a h) c", c=64)
        vP2 = pC[:, 2:4, :].rearrange("p a (h c) -> p (a h) c", c=64)
        vZ = pC[:, 4:6, :].rearrange("p a (h c) -> p (a h) c", c=64)
        vY = pC[:, 5, :].rearrange("p (k c) -> p k c", c=64)
        vY2 = pC[:, 3, :].rearrange("p (k c) -> p k c", c=64)
        vS = pC[:, 4, :].rearrange("p (k c) -> p k c", c=64)

        with nc.Block() as block:
            @block.gpsimd
            def _(g):
                g.wait_ge(s_m.h, s_m.base)
                g.wait_ge(s_st.h, s_st.base)
                for nm, dst in (("L1_attn_Wq", wq_sb), ("L1_attn_Wk", wk_sb),
                                ("L1_attn_Wv", wv_sb), ("L1_attn_Wo", wo_sb)):
                    g.dma_start(dst[:],
                                d[nm].rearrange("(ko ki) d -> ki ko d", ki=128)
                                ).then_inc(s_ldg.h, 16)
                g.wait_ge(s_ldg.h, s_ldg.base + 64)

            @block.sync
            def _(sync):
                sync.dma_start(ctril[:], d["c_tril"]).then_inc(s_ld.h, 16)
                sync.dma_start(ctriu[:], d["c_triu"]).then_inc(s_ld.h, 16)
                sync.dma_start(ch2[:], d["c_h2"]).then_inc(s_ld.h, 16)
                sync.dma_start(cs2sel[:], d["c_s2"]).then_inc(s_ld.h, 16)
                sync.dma_start(cS1[:], d["c_selS1"]).then_inc(s_ld.h, 16)
                sync.dma_start(cM16[:], d["c_m16x8"]).then_inc(s_ld.h, 16)
                sync.dma_start(con8[:], d["c_ones_row"][0:1, 0:8]
                               ).then_inc(s_ld.h, 16)
                sync.dma_start(cabr[0:1, 0, 0:16],
                               d["L1_attn_alpha_raw"].rearrange(
                                   "a b c e -> a (b c e)")).then_inc(s_ld.h, 16)
                sync.dma_start(cabr[0:1, 1, 0:16],
                               d["L1_attn_beta_raw"].rearrange(
                                   "a b c e -> a (b c e)")).then_inc(s_ld.h, 16)
                sync.wait_ge(s_ld.h, s_ld.base + 9 * 16)
        s_ld.base += 9 * 16
        s_ldg.base += 64

        E = []
        st = 0

        def f_sig(s):
            s.wait_ge(s_ld.h, s_ld.base)
            return s.activation(cabr[0:1, 0:2, 0:16]
                                .rearrange("a b c -> a (b c)"),
                                cabr[0:1, 0:2, 0:16]
                                .rearrange("a b c -> a (b c)"), AF.Sigmoid)
        E.append((st, "scalar", f_sig)); st += 1

        def f_half(v):
            return v.tensor_scalar(cabr[0:1, 0:2, 0:16]
                                   .rearrange("a b c -> a (b c)"),
                                   cabr[0:1, 0:2, 0:16]
                                   .rearrange("a b c -> a (b c)"),
                                   0.5, None, ALU.mult)
        E.append((st, "vector", f_half)); st += 1

        for i in range(2):
            def f_rep(te, i=i):
                if i == 0:
                    te.wait_ge(s_ld.h, s_ld.base)
                return te.matmul(pC[0:16, 0, 0:8], cabr[0:1, i, 0:16],
                                 con8[:], start=True, stop=True)
            E.append((st, "tensor", f_rep)); st += 1

            def f_repe(v, i=i):
                ins = v.tensor_copy(crep[:, i, :], pC[0:16, 0, 0:8])
                ve.edge(v, ins)
                return v.tensor_mul(crep[:, i, :], crep[:, i, :], cM16[:])
            E.append((st, "vector", f_repe)); st += 1

            def f_sel(te, i=i):
                return te.matmul(pC[0:128, 1, 0:8], cS1[:], crep[:, i, :],
                                 start=True, stop=True)
            E.append((st, "tensor", f_sel)); st += 1

            def f_sele(v, i=i):
                return v.tensor_copy(cab[:, i, :], pC[:, 1, 0:8])
            E.append((st, "vector", f_sele)); st += 1
        run_plan(nc, s_m, E)

        ld_sg = 8 * 16
        st_g = 4 * 16
        for g in range(NG):
            bidx, seg = divmod(g, NGRP // BLOC)
            r0 = g * GRP
            E = []
            st = 0

            def f_lx(sync, r0=r0):
                for dt in range(8):
                    sync.dma_start(cxt[:, dt, :], d["z_xn2t"][dt][:, r0:r0 + GRP]
                                   ).then_inc(s_ld.h, 16)
                return None
            E.append((st, "sync", f_lx))

            if seg == 0:
                def f_rst(v):
                    v.memset(cS[:], 0.0)
                    return v.memset(cSb[:], 0.0)
                E.append((st, "vector", f_rst)); st += 1

            for dt in range(8):
                def f_pq(te, dt=dt, g=g):
                    if dt == 0:
                        if g == 0:
                            te.wait_ge(s_ldg.h, s_ldg.base)
                        te.wait_ge(s_ld.h, s_ld.base + ld_sg * g + 8 * 16)
                    for kt in range(8):
                        ins = te.matmul(pC[:, dt % 2, :],
                                        wq_sb[:, kt, dt * 128:(dt + 1) * 128],
                                        cxt[:, kt, :], start=(kt == 0),
                                        stop=(kt == 7))
                    return ins
                E.append((st, "tensor", f_pq)); st += 1

                def f_pqe(v, dt=dt):
                    return v.tensor_copy(qkv[:, dt, 0, :], pC[:, dt % 2, :])
                E.append((st + 1, "vector", f_pqe))

                def f_pv(te, dt=dt):
                    for kt in range(8):
                        ins = te.matmul(pC[:, 2 + dt % 2, :],
                                        wv_sb[:, kt, dt * 128:(dt + 1) * 128],
                                        cxt[:, kt, :], start=(kt == 0),
                                        stop=(kt == 7))
                    return ins
                E.append((st + 1, "tensor", f_pv)); st += 1

                def f_pve(v, dt=dt):
                    return v.tensor_scalar(qkv[:, dt, 3, :], pC[:, 2 + dt % 2, :],
                                           cab[:, 1, dt:dt + 1], None, ALU.mult)
                E.append((st + 1, "vector", f_pve))
                st += 1

            for half in range(4) if 'knorm' not in PC_SKIP else []:
                for dt4 in range(2):
                    dt = half * 2 + dt4

                    def f_pk(te, dt=dt):
                        for kt in range(8):
                            ins = te.matmul(pC[:, dt % 2, :],
                                            wk_sb[:, kt,
                                                  dt * 128:(dt + 1) * 128],
                                            cxt[:, kt, :], start=(kt == 0),
                                            stop=(kt == 7))
                        return ins
                    E.append((st, "tensor", f_pk)); st += 1

                    def f_pke(v, dt=dt, dt4=dt4):
                        return v.tensor_copy(ckt4[:, dt4, :], pC[:, dt % 2, :])
                    E.append((st, "vector", f_pke))
                st += 1

                def f_ksq4(s):
                    return s.activation(
                        cksq4[:].rearrange("p a b -> p (a b)"),
                        ckt4[:].rearrange("p a b -> p (a b)"), AF.Square)
                E.append((st, "scalar", f_ksq4)); st += 1

                def f_ssq4(te):
                    ins = None
                    for j in range(2):
                        ins = te.matmul(pC[0:2, j, :], ch2[:],
                                        cksq4[:, j, :], start=True, stop=True,
                                        skip_group_check=True)
                    return ins
                E.append((st, "tensor", f_ssq4)); st += 1

                def f_rn4(v):
                    return v.tensor_scalar_max(
                        crn4[0:2, 0, :],
                        pC[0:2, 0:2, :].rearrange("p a b -> p (a b)"), 1e-24)
                E.append((st, "vector", f_rn4)); st += 1

                def f_rsq4(s):
                    return s.activation(crn4[0:2, 1, :], crn4[0:2, 0, :],
                                        AF.Sqrt)
                E.append((st, "scalar", f_rsq4)); st += 1

                def f_rnw4(v):
                    v.reciprocal(crn4[0:2, 2, :], crn4[0:2, 1, :])
                    v.tensor_mul(crn4[0:2, 1, :], crn4[0:2, 2, :],
                                 crn4[0:2, 2, :])
                    v.tensor_mul(crn4[0:2, 1, :], crn4[0:2, 1, :],
                                 crn4[0:2, 0, :])
                    v.tensor_scalar(crn4[0:2, 1, :], crn4[0:2, 1, :], -0.5,
                                    1.5, ALU.mult, ALU.add)
                    return v.tensor_mul(crn4[0:2, 2, :], crn4[0:2, 2, :],
                                        crn4[0:2, 1, :])
                E.append((st, "vector", f_rnw4)); st += 1

                def f_bcn4(te):
                    ins = None
                    for j in range(2):
                        ins = te.matmul(pC[0:128, j, :], cs2sel[:],
                                        crn4[0:2, 2, j * GRP:(j + 1) * GRP],
                                        start=True, stop=True,
                                        skip_group_check=True)
                    return ins
                E.append((st, "tensor", f_bcn4)); st += 1

                def f_kn4(v, half=half):
                    d0 = half * 2
                    ins = v.tensor_mul(qkv[:, d0:d0 + 2, 1, :],
                                       ckt4[:], pC[:, 0:2, :])
                    ve.edge(v, ins)
                    for dt4 in range(2):
                        dt = d0 + dt4
                        ins = v.tensor_scalar(qkv[:, dt, 2, :],
                                              qkv[:, dt, 1, :],
                                              cab[:, 0, dt:dt + 1], None,
                                              ALU.mult)
                    return ins
                E.append((st, "vector", f_kn4)); st += 1

            for tile in range(4) if 'trans' not in PC_SKIP else []:
                for kt in range(8):
                    for half in range(2):
                        hh = half * 64

                        i_tr = tile * 16 + kt * 2 + half

                        def f_tr(te, tile=tile, kt=kt, half=half, hh=hh,
                                 i_tr=i_tr):
                            bk = i_tr % 2
                            te.transpose(
                                pT[:, bk, 0:64],
                                qkv[hh:hh + 64, kt, 1,
                                    tile * 128:(tile + 1) * 128],
                                identb[hh:hh + 64, hh:hh + 64],
                                tile_position=(hh, 0))
                            return te.transpose(
                                pT[:, bk, 64:128],
                                qkv[hh:hh + 64, kt, 3,
                                    tile * 128:(tile + 1) * 128],
                                identb[hh:hh + 64, hh:hh + 64],
                                tile_position=(hh, 0))
                        E.append((st, "tensor", f_tr))

                        def f_tre(v, tile=tile, kt=kt, half=half, i_tr=i_tr):
                            bk = i_tr % 2
                            v.tensor_copy(krows[:, half, kt, tile, :],
                                          pT[:, bk, 0:64])
                            return v.tensor_copy(vrows[:, half, kt, tile, :],
                                                 pT[:, bk, 64:128])
                        E.append((st + 1, "vector", f_tre)); st += 1

            for cp in range(4) if 'chain' not in PC_SKIP else []:
                c0 = cp * 128

                def f_kkkq(te, cp=cp, c0=c0):
                    if "kkkq" in PC_SKIP:
                        return te.matmul(pC[:, 0, :], wq_sb[:, 0, 0:128],
                                         cxt[:, 0, 0:512], start=True,
                                         stop=True)
                    mms = []
                    for h in range(16):
                        kt, half = h // 2, h % 2
                        hh = half * 64
                        s_ = half * 8 + kt
                        for ci in range(2):
                            cc = ci * 64
                            csl = slice(c0 + cc, c0 + cc + 64)
                            mms.append((vKK[cc:cc + 64, s_, :],
                                        qkv[hh:hh + 64, kt, 2, csl],
                                        qkv[hh:hh + 64, kt, 1, csl],
                                        (hh, cc), ("kk", s_, cc)))
                            mms.append((vKQ[cc:cc + 64, s_, :],
                                        qkv[hh:hh + 64, kt, 1, csl],
                                        qkv[hh:hh + 64, kt, 0, csl],
                                        (hh, cc), ("kq", s_, cc)))
                    return mm_group(te, mms)
                E.append((st, "tensor", f_kkkq)); st += 1

                def f_mask(v, cp=cp):
                    tri3 = ctril[:, None, :].broadcast_to([128, 16, 64])
                    triu3 = ctriu[:, None, :].broadcast_to([128, 16, 64])
                    v.tensor_tensor(cNp[:, :, 0, :], vKK[:], tri3, ALU.mult)
                    v.tensor_tensor(cP6[:, :, 0, :], vKK[:], triu3, ALU.mult)
                    return v.tensor_tensor(cLt[:], vKQ[:], triu3, ALU.mult)
                E.append((st, "vector", f_mask)); st += 1

                for m in range(5):
                    def f_sqr(te, m=m):
                        if "sqr" in PC_SKIP:
                            return te.matmul(pC[:, 0, :], wq_sb[:, 0, 0:128],
                                             cxt[:, 0, 0:512], start=True,
                                             stop=True)
                        mms = []
                        for h in range(16):
                            for ci in range(2):
                                cc = ci * 64
                                nsrc = cNp[cc:cc + 64, h, m % 2, :]
                                psrc = cP6[cc:cc + 64, h, m, :]
                                mms.append((vN2[cc:cc + 64, h, :], psrc, nsrc,
                                            (cc, cc), ("n2", h // 8, cc)))
                                mms.append((vP2[cc:cc + 64, h, :], nsrc, psrc,
                                            (cc, cc), ("p2", h // 8, cc)))
                        return mm_group(te, mms)
                    E.append((st, "tensor", f_sqr)); st += 1

                    def f_sqe(v, m=m):
                        v.tensor_copy(cNp[:, :, (m + 1) % 2, :], vN2[:])
                        return v.tensor_copy(cP6[:, :, m + 1, :], vP2[:])
                    E.append((st, "vector", f_sqe)); st += 1

                for ci in range(2) if 'spass' not in PC_SKIP else []:
                    c = 2 * cp + ci
                    cc = ci * 64
                    csl = slice(c0 + cc, c0 + cc + 64)

                    def f_z(te, ci=ci, cc=cc, csl=csl):
                        if "z" in PC_SKIP:
                            return te.matmul(pC[:, 0, :], wq_sb[:, 0, 0:128],
                                             cxt[:, 0, 0:512], start=True,
                                             stop=True)
                        mms = []
                        for h in range(16):
                            kt, half = h // 2, h % 2
                            hh = half * 64
                            s_ = half * 8 + kt
                            mms.append((vZ[cc:cc + 64, s_, :],
                                        qkv[hh:hh + 64, kt, 2, csl],
                                        cSb[hh:hh + 64, kt, :],
                                        (hh, cc), ("z", s_, cc)))
                        return mm_group(te, mms)
                    E.append((st, "tensor", f_z)); st += 1

                    def f_zv(v, cp=cp, ci=ci, cc=cc):
                        vr = vrows[cc:cc + 64, :, :, cp, :].rearrange(
                            "p a b c -> p (a b) c")
                        return v.tensor_sub(cR[cc:cc + 64, :, :], vr,
                                            vZ[cc:cc + 64, :, :])
                    E.append((st, "vector", f_zv)); st += 1

                    for m in range(6):
                        def f_ap(te, m=m, cc=cc):
                            if "ap" in PC_SKIP:
                                return te.matmul(pC[:, 0, :],
                                                 wq_sb[:, 0, 0:128],
                                                 cxt[:, 0, 0:512], start=True,
                                                 stop=True)
                            mms = []
                            for s_ in range(16):
                                mms.append((vZ[cc:cc + 64, s_, :],
                                            cP6[cc:cc + 64, s_, m, :],
                                            cR[cc:cc + 64, s_, :],
                                            (cc, cc), ("ap", s_, cc)))
                            return mm_group(te, mms)
                        E.append((st, "tensor", f_ap)); st += 1

                        def f_apv(v, m=m, cc=cc):
                            return v.tensor_tensor(
                                cR[cc:cc + 64, :, :], cR[cc:cc + 64, :, :],
                                vZ[cc:cc + 64, :, :],
                                ALU.subtract if m == 0 else ALU.add)
                        E.append((st, "vector", f_apv)); st += 1

                    def f_y(te, ci=ci, cc=cc, csl=csl):
                        if "y" in PC_SKIP:
                            return te.matmul(pC[:, 0, :], wq_sb[:, 0, 0:128],
                                             cxt[:, 0, 0:512], start=True,
                                             stop=True)
                        mms = []
                        for h in range(16):
                            kt, half = h // 2, h % 2
                            hh = half * 64
                            s_ = half * 8 + kt
                            mms.append((vZ[cc:cc + 64, s_, :],
                                        qkv[hh:hh + 64, kt, 0, csl],
                                        cSb[hh:hh + 64, kt, :],
                                        (hh, cc), ("y1", s_, cc)))
                        for s_ in range(16):
                            mms.append((vKQ[cc:cc + 64, s_, :],
                                        cLt[cc:cc + 64, s_, :],
                                        cR[cc:cc + 64, s_, :],
                                        (cc, cc), ("y2", s_, cc)))
                        return mm_group(te, mms)
                    E.append((st, "tensor", f_y)); st += 1

                    def f_ye(v, cp=cp, cc=cc):
                        for half in range(2):
                            o = cyN[cc:cc + 64, cp, :, half, :]
                            ins = v.tensor_copy(
                                o, vZ[cc:cc + 64, half * 8:half * 8 + 8, :])
                            ve.edge(v, ins)
                            ins = v.tensor_tensor(
                                o, o, vKQ[cc:cc + 64, half * 8:half * 8 + 8,
                                          :], ALU.add)
                        return ins
                    E.append((st, "vector", f_ye)); st += 1

                    def f_su(te, cp=cp, ci=ci, cc=cc):
                        if "su" in PC_SKIP:
                            return te.matmul(pC[:, 0, :], wq_sb[:, 0, 0:128],
                                             cxt[:, 0, 0:512], start=True,
                                             stop=True)
                        mms = []
                        for h in range(16):
                            kt, half = h // 2, h % 2
                            hh = half * 64
                            s_ = half * 8 + kt
                            mms.append((vS[hh:hh + 64, kt, :],
                                        krows[cc:cc + 64, half, kt, cp, :],
                                        cR[cc:cc + 64, s_, :],
                                        (cc, hh), ("s", hh)))
                        return mm_group(te, mms)
                    E.append((st, "tensor", f_su)); st += 1

                    def f_sv(v):
                        ins = v.tensor_add(cS[:], cS[:], vS[:])
                        ve.edge(v, ins)
                        return v.tensor_copy(cSb[:], cS[:])
                    E.append((st, "vector", f_sv)); st += 1

            for cp2 in range(4) if 'chain' not in PC_SKIP else []:
                for dt in range(8):
                    i2 = cp2 * 8 + dt

                    def f_tr2(te, cp2=cp2, dt=dt, i2=i2):
                        return te.transpose(
                            pT[:, i2 % 2, 0:128],
                            cyN[:, cp2, :, :, :].rearrange(
                                "p a b c -> p (a b c)")[:,
                                                        dt * 128:
                                                        (dt + 1) * 128],
                            identb[:])
                    E.append((st, "tensor", f_tr2))

                    def f_tre2(v, cp2=cp2, dt=dt, i2=i2):
                        return v.tensor_copy(
                            cyt[:, dt, cp2 * 128:(cp2 + 1) * 128],
                            pT[:, i2 % 2, 0:128])
                    E.append((st + 1, "vector", f_tre2)); st += 1
            st += 1

            for t in range(4):
                def f_wo(te, t=t):
                    for kt in range(8):
                        for ncc in range(2):
                            ins = te.matmul(pC[:, ncc, :],
                                            cyt[:, kt, t * 128:(t + 1) * 128],
                                            wo_sb[:, kt, ncc * 512:
                                                  (ncc + 1) * 512],
                                            start=(kt == 0), stop=(kt == 7))
                    return ins
                E.append((st, "tensor", f_wo)); st += 1

                def f_woe(v, t=t, g=g, blkst=st):
                    if t == 0 and g >= 1:
                        v.wait_ge(s_st.h, s_st.base + st_g * g)
                    return v.tensor_copy(
                        cs2[:, t, :],
                        pC[:, 0:2, :].rearrange("p a b -> p (a b)"))
                E.append((st, "vector", f_woe)); st += 1

                def f_st2(sync, t=t, r0=r0):
                    rt = r0 + t * 128
                    sync.dma_start(d["z_s2n"][rt:rt + 128, :], cs2[:, t, :]
                                   ).then_inc(s_st.h, 16)
                    return None
                E.append((st, "sync", f_st2)); st += 1
            run_plan(nc, s_m, E)

        s_ld.base += ld_sg * NG
        s_st.base += st_g * NG
        with nc.Block() as block:
            @block.sync
            def _(sync):
                sync.wait_ge(s_st.h, s_st.base)


def emit_phaseD(ctx):
    nc = ctx.nc
    d = ctx.d
    sm = ctx.sems
    s_m, s_ld, s_ldg, s_st = sm["m"], sm["ld"], sm["ldg"], sm["st"]
    ve = sm["ev2"]
    identb = ctx.identb
    NT = S // 128
    NCH = [(i * 512, min(512, DH2 - i * 512)) for i in range((DH2 + 511) // 512)]
    HQC = [(0, 1024), (1024, 1024), (2048, 768)]
    KD = DH2 // 128

    specs = [
        ("d_wg", [128, 8, DH2], BF16),
        ("d_wu", [128, 8, DH2], BF16),
        ("d_wd", [128, KD, D], BF16),
        ("d_xn", [128, D], F32),
        ("d_xq", [128, D], BF16),
        ("d_xqT", [128, 8, 128], BF16),
        ("d_h", [128, DH2], F32),
        ("d_hq", [128, DH2], BF16),
        ("d_hqT", [128, KD, 128], BF16),
        ("d_t0", [128, 1024], F32),
        ("d_t1", [128, 1024], F32),
        ("d_t2", [128, 1024], F32),
        ("d_t3", [128, 1024], F32),
        ("d_t4", [128, 1024], F32),
        ("d_sc", [128, 3, 176], F32),
    ]
    with contextlib.ExitStack() as stack:
        tl = {s_[0]: stack.enter_context(nc.sbuf_tensor(*s_)) for s_ in specs}
        pD = stack.enter_context(nc.psum_tensor("d_pD", [128, 4, 512], F32))
        pTd = stack.enter_context(nc.psum_tensor("d_pT", [128, 2, 1024], BF16))
        (wg_sb, wu_sb, wd_sb, dxn, dxq, dxqT, dh, dhq, dhqT,
         dt0, dt1, dt2, dt3, dt4, dsc) = (tl[s_[0]] for s_ in specs)
        tmp = {"xa": dt0, "a3": dt0, "sgn": dt1, "t": dt2, "a1": dt3,
               "a2": dt4, "rmax": dsc[:, 0, :], "scale": dsc[:, 1, :],
               "inv": dsc[:, 2, :]}

        with nc.Block() as block:
            @block.gpsimd
            def _(g):
                g.wait_ge(s_m.h, s_m.base)
                g.wait_ge(s_st.h, s_st.base)
                g.dma_start(wg_sb[:], d["z_wgt"].rearrange(
                    "(ko ki) n -> ki ko n", ki=128)).then_inc(s_ldg.h, 16)
                g.dma_start(wu_sb[:], d["z_wut"].rearrange(
                    "(ko ki) n -> ki ko n", ki=128)).then_inc(s_ldg.h, 16)
                g.dma_start(wd_sb[:], d["z_wdt"].rearrange(
                    "(ko ki) n -> ki ko n", ki=128)).then_inc(s_ldg.h, 16)
                g.wait_ge(s_ldg.h, s_ldg.base + 48)
        s_ldg.base += 48

        for t in range(NT):
            rt = t * 128
            E = []
            st = 0

            def f_lx(sync, rt=rt):
                sync.dma_start(dxn[:], d["z_xn2n"][rt:rt + 128, :]
                               ).then_inc(s_ld.h, 16)
                return None
            E.append((st, "sync", f_lx))

            stg = 16 + (64 if PD_DBG else 0)

            def f_qs(s, t=t, stg=stg):
                s.wait_ge(s_ld.h, s_ld.base + 48 * t + 16)
                if t >= 1:
                    s.wait_ge(s_st.h, s_st.base + stg * t)
                return qdq_stage1_s(s, dxn[:], tmp, D)
            E.append((st, "scalar", f_qs)); st += 1

            def f_qv(v, t=t, stg=stg):
                if PD_DBG and t >= 1:
                    v.wait_ge(s_st.h, s_st.base + stg * t)
                return qdq_vector(v, dxn[:], dxq[:], tmp, D, ve)
            E.append((st, "vector", f_qv)); st += 1

            for k in range(8):
                def f_xt(te, k=k):
                    return te.transpose(pTd[:, k % 2, 0:128],
                                        dxq[:, k * 128:(k + 1) * 128],
                                        identb[:])
                E.append((3 + k, "tensor", f_xt))

                def f_xte(s, k=k):
                    return s.copy(dxqT[:, k, :], pTd[:, k % 2, 0:128])
                E.append((4 + k, "scalar", f_xte))

            ht_st = 0
            for ch, (n0, w) in enumerate(NCH):
                G = 12 + 3 * ch

                def f_g(te, n0=n0, w=w, t=t, ch=ch):
                    if t == 0 and ch == 0:
                        te.wait_ge(s_ldg.h, s_ldg.base)
                    for k in range(8):
                        ins = te.matmul(pD[:, 0, 0:w], dxqT[:, k, :],
                                        wg_sb[:, k, n0:n0 + w],
                                        start=(k == 0), stop=(k == 7),
                                        skip_group_check=True)
                    return ins
                E.append((G, "tensor", f_g))

                def f_u(te, n0=n0, w=w):
                    for k in range(8):
                        ins = te.matmul(pD[:, 1, 0:w], dxqT[:, k, :],
                                        wu_sb[:, k, n0:n0 + w],
                                        start=(k == 0), stop=(k == 7),
                                        skip_group_check=True)
                    return ins
                E.append((G + 1, "tensor", f_u))

                def f_ge(s, n0=n0, w=w):
                    return s.activation(dh[:, n0:n0 + w], pD[:, 0, 0:w],
                                        AF.Silu)
                E.append((G + 2, "scalar", f_ge))

                def f_he(v, n0=n0, w=w):
                    return v.tensor_tensor(dh[:, n0:n0 + w], dh[:, n0:n0 + w],
                                           pD[:, 1, 0:w], ALU.mult)
                E.append((G + 3, "vector", f_he))

                if ch % 2 == 1:
                    c = ch // 2
                    c0, cw = HQC[c]

                    def f_hs(s, c0=c0, cw=cw):
                        return qdq_stage1_s(s, dh[:, c0:c0 + cw], tmp, cw)
                    E.append((G + 4, "scalar", f_hs))

                    def f_hv(v, c0=c0, cw=cw):
                        return qdq_vector(v, dh[:, c0:c0 + cw],
                                          dhq[:, c0:c0 + cw], tmp, cw, ve)
                    E.append((G + 5, "vector", f_hv))

                    for k in range(c0 // 128, (c0 + cw) // 128):
                        ht_st = max(ht_st + 1, G + 6)

                        def f_ht(te, k=k):
                            return te.transpose(pTd[:, k % 2, 0:128],
                                                dhq[:, k * 128:(k + 1) * 128],
                                                identb[:])
                        E.append((ht_st, "tensor", f_ht))

                        def f_hte(s, k=k):
                            return s.copy(dhqT[:, k, :], pTd[:, k % 2, 0:128])
                        E.append((ht_st + 1, "scalar", f_hte))
            st = ht_st + 2

            if PD_DBG:
                def f_dbg(sync, rt=rt):
                    sync.dma_start(d["z_dxq"][rt:rt + 128, :], dxq[:]
                                   ).then_inc(s_st.h, 16)
                    sync.dma_start(d["z_dh"][rt:rt + 128, :], dh[:]
                                   ).then_inc(s_st.h, 16)
                    sync.dma_start(d["z_dhq"][rt:rt + 128, :], dhq[:]
                                   ).then_inc(s_st.h, 16)
                    return None
                E.append((st, "sync", f_dbg))

            def f_lx2(sync, rt=rt):
                sync.dma_start(dt0[:], d["z_x2n"][rt:rt + 128, :]
                               ).then_inc(s_ld.h, 16)
                sync.dma_start(dt1[:], d["z_s2n"][rt:rt + 128, :]
                               ).then_inc(s_ld.h, 16)
                return None
            E.append((st, "sync", f_lx2))

            for ci in range(2):
                def f_d(te, ci=ci):
                    for k in range(KD):
                        ins = te.matmul(pD[:, 2 + ci, :], dhqT[:, k, :],
                                        wd_sb[:, k, ci * 512:(ci + 1) * 512],
                                        start=(k == 0), stop=(k == KD - 1),
                                        skip_group_check=True)
                    return ins
                E.append((st, "tensor", f_d)); st += 1

            def f_de(s, t=t):
                s.wait_ge(s_ld.h, s_ld.base + 48 * t + 48)
                s.activation(dxn[:, 0:512], pD[:, 2, :], AF.Silu)
                return s.activation(dxn[:, 512:1024], pD[:, 3, :], AF.Silu)
            E.append((st, "scalar", f_de)); st += 1

            def f_cmb(v):
                v.tensor_tensor(dt1[:], dt1[:], dxn[:], ALU.mult)
                return v.tensor_tensor(dt0[:], dt0[:], dt1[:], ALU.add)
            E.append((st, "vector", f_cmb)); st += 1

            E.sort(key=lambda e: e[0])
            def f_st(sync, rt=rt):
                if PD_DBG:
                    sync.dma_start(d["z_dg2"][rt:rt + 128, :], dxn[:]
                                   ).then_inc(s_st.h, 16)
                sync.dma_start(d["out"][rt:rt + 128, :], dt0[:]
                               ).then_inc(s_st.h, 16)
                return None
            E.append((st, "sync", f_st)); st += 1
            run_plan(nc, s_m, E)

        s_ld.base += 48 * NT
        s_st.base += (16 + (64 if PD_DBG else 0)) * NT
        with nc.Block() as block:
            @block.sync
            def _(sync):
                sync.wait_ge(s_st.h, s_st.base)


def declare_dram(nc, phases, scratch_kinds):
    d = {}
    d["x"] = nc.dram_tensor("x", list(INPUT_SHAPES["x"]), F32,
                            kind="ExternalInput").ap()
    for n in WEIGHT_NAMES:
        d[n] = nc.dram_tensor(n, list(INPUT_SHAPES[n]), F32,
                              kind="ExternalInput").ap()
    for n, arr in host_constants().items():
        dt = CONST_DTYPES.get(n, F32)
        d[n] = nc.dram_tensor(n, list(arr.shape), dt, kind="ExternalInput").ap()
    for n, (shp, dt) in SCRATCH.items():
        kind = scratch_kinds.get(n)
        if kind is None:
            d[n] = nc.dram_tensor(n, list(shp), dt).ap()
        else:
            d[n] = nc.dram_tensor(n, list(shp), dt, kind=kind).ap()
    d["out"] = nc.dram_tensor("out", [S, D], F32, kind="ExternalOutput").ap()
    return d


PHASE_FNS = {}


def build_program(phases=("P0", "PA", "PB", "PC", "PD"), scratch_out=(),
                  scratch_in=()):
    nc = bass.Bass("TRN2", target_bir_lowering=False)
    kinds = {n: "ExternalOutput" for n in scratch_out}
    kinds.update({n: "ExternalInput" for n in scratch_in})
    d = declare_dram(nc, phases, kinds)
    ctx = Ctx(nc, d)
    with (
        nc.sbuf_tensor("g_identb", [128, 128], BF16) as identb,
        nc.semaphore("sm_m") as h_m,
        nc.semaphore("sm_ld") as h_ld,
        nc.semaphore("sm_ldg") as h_ldg,
        nc.semaphore("sm_st") as h_st,
        nc.semaphore("sm_tp") as h_tp,
        nc.semaphore("sm_ev") as h_ev,
        nc.semaphore("sm_q") as h_q,
        nc.semaphore("sm_q2") as h_q2,
        nc.semaphore("sm_ev2") as h_ev2,
        nc.semaphore("sm_es") as h_es,
        nc.semaphore("sm_et") as h_et,
    ):
        ctx.identb = identb
        ctx.sems = {"m": SemW(h_m), "ld": SemW(h_ld), "ldg": SemW(h_ldg),
                    "st": SemW(h_st), "tp": SemW(h_tp), "ev": SemW(h_ev),
                    "q": SemW(h_q), "q2": SemW(h_q2),
                    "ev2": SemW(h_ev2), "es": SemW(h_es), "et": SemW(h_et)}
        with nc.Block() as block:
            @block.sync
            def _(sync):
                sync.dma_start(identb[:], d["c_ident"]
                               ).then_inc(ctx.sems["ld"].h, 16)
                sync.wait_ge(ctx.sems["ld"].h, 16)
        ctx.sems["ld"].base += 16
        for ph in phases:
            with nc.named_scope(ph):
                PHASE_FNS[ph](ctx)
    return nc


def stage_inputs(inputs):
    import ml_dtypes
    x = np.asarray(inputs["L0_attn_x"])
    consts = host_constants()
    maps = []
    for c in range(NCORES):
        m = {}
        m["x"] = np.ascontiguousarray(
            x[c * BLOC:(c + 1) * BLOC].reshape(S, D))
        for n in WEIGHT_NAMES:
            m[n] = np.ascontiguousarray(np.asarray(inputs[n]))
        for n, arr in consts.items():
            if CONST_DTYPES.get(n) == BF16:
                m[n] = arr.astype(ml_dtypes.bfloat16)
            else:
                m[n] = arr
        maps.append(m)
    return maps


def kernel(**inputs):
    nc = build_program()
    maps = stage_inputs(inputs)
    res = run_bass_kernel_spmd(nc, maps, core_ids=list(range(NCORES)))
    out = np.stack([r["out"].reshape(BLOC, T, D) for r in res.results])
    return np.ascontiguousarray(out.reshape(NCORES * BLOC, T, D))


def _register_phases():
    PHASE_FNS["P0"] = emit_prep
    PHASE_FNS["PA"] = emit_phaseA
    PHASE_FNS["PB"] = emit_phaseB
    PHASE_FNS["PC"] = emit_phaseC
    PHASE_FNS["PD"] = emit_phaseD


_register_phases()
